# version 04
# baseline (speedup 1.0000x reference)
import numpy as np
from contextlib import ExitStack

import concourse.bass as bass
import concourse.mybir as mybir
from concourse.bass_utils import run_bass_kernel_spmd

F32 = mybir.dt.float32
F16 = mybir.dt.float16
BF16 = mybir.dt.bfloat16
AF = mybir.ActivationFunctionType
Alu = mybir.AluOpType

C = 2048
T = 8192
BATCH = 2
NCORES = 8
TC = 512
NF = C // 128
NCHUNK = 5
ROWS = NCHUNK * TC
HID = 8192
NH = HID // 128
EPS = 1e-5

ENGINES = ("sync", "gpsimd", "tensor", "vector", "scalar")

DBG = None

OFF_X = 0
OFF_S1 = 8192
OFF_S2 = 16384
OFF_S3 = 24576
OFF_S4 = 32768
OFF_WB = 40960
OFF_A = 45056
OFF_B = 45569
OFF_STAGE = 46082
OFF_SQ = 48130
ARENA = 49154


class Sched:
    def __init__(self):
        self.ops = {e: [] for e in ENGINES}
        self.cnt = {}

    STRICT = {"vector": "eV", "scalar": "eS"}

    def emit(self, eng, fn, waits=(), inc=None, amt=1, noinc=False):
        ws = [(s, v) for (s, v) in waits if v and v > 0]
        own = self.STRICT.get(eng)
        if own is not None and not noinc:
            prev = self.cnt.get(own, 0)
            if prev > 0:
                ws.append((own, prev))
            if inc is None:
                inc, amt = own, 1
            else:
                assert inc == own, (eng, inc)
        self.ops[eng].append((fn, tuple(ws), inc, amt))
        if inc is not None:
            self.cnt[inc] = self.cnt.get(inc, 0) + amt
            return self.cnt[inc]
        return None

    def now(self, sem):
        return self.cnt.get(sem, 0)

    def replay(self, engname, eng, sems):
        waited = {}
        for fn, waits, inc, amt in self.ops[engname]:
            for s, v in waits:
                if waited.get(s, 0) >= v:
                    continue
                eng.wait_ge(sems[s], v)
                waited[s] = v
            ins = fn(eng)
            if inc is not None:
                ins.then_inc(sems[inc], amt)


def build_program():
    nc = bass.Bass(trn_type="TRN2", target_bir_lowering=False,
                   detect_race_conditions=False)

    d_xs = nc.dram_tensor("xs", [ROWS, C], F32, kind="ExternalInput").ap()
    d_y = nc.dram_tensor("y", [4 * TC, C], F32, kind="ExternalOutput").ap()
    wp = {}
    for nm in ("wr", "wk", "wv", "wg", "wo"):
        wp[nm] = nc.dram_tensor(nm, [NF, 128, C], F32, kind="ExternalInput").ap()
    wp["w0"] = nc.dram_tensor("w0", [NH, 128, C], F32, kind="ExternalInput").ap()
    wp["w1"] = nc.dram_tensor("w1", [NF, 128, HID], F32, kind="ExternalInput").ap()
    d_wmod = nc.dram_tensor("wmod", [C, 6 * C], F32, kind="ExternalInput").ap()
    d_crow = nc.dram_tensor("crow", [1, C], F32, kind="ExternalInput").ap()
    d_cols = {}
    for nm in ("td", "tf", "tmk", "tmv", "tmr", "tmg", "lng", "lnb", "b1"):
        d_cols[nm] = nc.dram_tensor(nm, [128, NF], F32, kind="ExternalInput").ap()
    d_b0 = nc.dram_tensor("b0", [128, NH], F32, kind="ExternalInput").ap()
    d_cflag = nc.dram_tensor("cflag", [128, 1], F32, kind="ExternalInput").ap()
    d_idn = nc.dram_tensor("idn", [128, 128], F32, kind="ExternalInput").ap()
    d_dbg = nc.dram_tensor("dbg", [128, 8192], F32, kind="ExternalOutput").ap()

    st = ExitStack()
    sb = lambda name, shape, dt=F32: st.enter_context(nc.sbuf_tensor(name, shape, dt))

    AR = sb("AR", [128, ARENA])
    IDN = sb("IDN", [128, 128])
    ONESC = sb("ONESC", [128, 1])
    ZEROC = sb("ZEROC", [128, 1])
    ONESR = sb("ONESR", [1, 128])
    COLS = {nm: sb("c_" + nm, [128, NF]) for nm in
            ("td", "tf", "tmk", "tmv", "tmr", "tmg", "lng", "lnb", "b1",
             "w", "eu", "sil", "tmp", "t2", "t3", "sa", "ca", "ga", "sb",
             "cb", "gb", "ca1", "cb1", "gax", "gbx", "gag", "gab",
             "cA", "cB", "hl")}
    B0C = sb("B0C", [128, NH])
    CFLAG = sb("CFLAG", [128, 1])
    GT = sb("GT", [128, 512])
    E16 = sb("E16", [128, 512], F16)

    PT = st.enter_context(nc.psum_tensor("PT", [128, 2048], F32))
    PU = st.enter_context(nc.psum_tensor("PU", [128, 2048], F32))

    sems = {s: st.enter_context(nc.semaphore(s))
            for s in ("dC", "dW0", "dW1", "dX", "dQ", "dG", "eT", "eV", "eS")}

    k = Sched()
    freed = {}

    def usew(*names):
        return [freed[n] for n in names if n in freed]

    def setfree(name, sem, cnt):
        freed[name] = (sem, cnt)

    def ar(off, n, p0=0, np_=128):
        return AR[p0:p0 + np_, off:off + n]

    X = lambda f, a=0, n=512: ar(OFF_X + f * 512 + a, n)
    S1 = lambda f, a=0, n=512: ar(OFF_S1 + f * 512 + a, n)
    S2 = lambda f, a=0, n=512: ar(OFF_S2 + f * 512 + a, n)
    S3 = lambda f, a=0, n=512: ar(OFF_S3 + f * 512 + a, n)
    S4 = lambda f, a=0, n=512: ar(OFF_S4 + f * 512 + a, n)
    WB = lambda par: ar(OFF_WB + par * 2048, 2048)
    SQ = lambda par: ar(OFF_SQ + par * 512, 512)
    STG = lambda: ar(OFF_STAGE, 2048)
    R_mu = ar(OFF_STAGE + 0, 512, 0, 1)
    R_v = ar(OFF_STAGE + 512, 512, 0, 1)
    R_t = ar(OFF_STAGE + 1024, 512, 0, 1)
    R_mr = ar(OFF_STAGE + 1536, 512, 0, 1)
    SCA = AR[:, OFF_A:OFF_A + 513]
    SCB = AR[:, OFF_B:OFF_B + 513]
    H2F = lambda f: ar(OFF_S4 + f * 256, 256).bitcast(F16)
    DF = lambda m: ar((OFF_S1 if m < 32 else OFF_S3 - 32 * 256) + m * 256,
                      256).bitcast(F16)
    WhV = lambda par: ar(OFF_WB + par * 2048, 1024).bitcast(BF16)
    WlV = lambda par: ar(OFF_WB + par * 2048 + 1024, 1024).bitcast(BF16)
    Mh = lambda f: ar(OFF_S2 + f * 256, 256).bitcast(BF16)
    Ml = lambda f: ar(OFF_S2 + 4096 + f * 256, 256).bitcast(BF16)
    ATTh = lambda f: ar(OFF_S1 + f * 512, 256).bitcast(BF16)
    ATTl = lambda f: ar(OFF_S1 + f * 512 + 256, 256).bitcast(BF16)
    WLAND = lambda par: ar(OFF_S2 + par * 2048, 2048)
    WHI = lambda par: ar(OFF_S2 + 4096 + par * 1024, 1024).bitcast(F16)
    WLO = lambda par: ar(OFF_S2 + 6144 + par * 1024, 1024).bitcast(F16)

    col = lambda nm, f: COLS[nm][:, f:f + 1]
    pt = lambda b, a=0, n=512: PT[:, b * 512 + a: b * 512 + a + n]
    pu = lambda b, a=0, n=512: PU[:, (b - 4) * 512 + a: (b - 4) * 512 + a + n]

    def dma_s(out, in_, sem="dC", waits=()):
        return k.emit("sync", lambda e, o=out, i=in_: e.dma_start(out=o, in_=i),
                      waits=waits, inc=sem, amt=16)

    for nm in ("td", "tf", "tmk", "tmv", "tmr", "tmg", "lng", "lnb", "b1"):
        dma_s(COLS[nm][:], d_cols[nm])
    dma_s(B0C[:], d_b0)
    dma_s(CFLAG[:], d_cflag)
    dma_s(IDN[:], d_idn)
    consts_dma = dma_s(STG()[0:1, :], d_crow)

    k.emit("vector", lambda e: e.memset(ONESC[:], 1.0))
    k.emit("vector", lambda e: e.memzero(ZEROC[:]))
    k.emit("vector", lambda e: e.memzero(COLS["cA"][:]))
    k.emit("vector", lambda e: e.memzero(COLS["cB"][:]))
    k.emit("vector", lambda e: e.memzero(COLS["hl"][:]), inc="eV")

    ev0 = k.emit("vector", lambda e: e.memset(ONESR[:], 1.0), inc="eV")
    lastt = None
    for t in range(NF):
        lastt = k.emit("tensor",
                       lambda e, t=t: e.matmul(PT[:, t:t + 1],
                                               STG()[0:1, t * 128:(t + 1) * 128],
                                               ONESR[0:1, 0:1],
                                               start=True, stop=True),
                       waits=[("dC", consts_dma), ("eV", ev0)] if t == 0 else (),
                       inc="eT" if t == NF - 1 else None)
    ev = k.emit("vector", lambda e: e.tensor_copy(COLS["tmp"][:], PT[:, 0:NF]),
                waits=[("eT", lastt)], inc="eV")
    es_sil = k.emit("scalar", lambda e: e.activation(COLS["sil"][:], COLS["tmp"][:],
                                                     AF.Silu),
                    waits=[("eV", ev)], inc="eS")

    for g, gname in enumerate(("sa", "ca", "ga", "sb", "cb", "gb")):
        lastmm = None
        for kc in range(NF):
            par = kc % 2
            nd = dma_s(WB(par)[:], d_wmod[kc * 128:(kc + 1) * 128, g * C:(g + 1) * C],
                       sem=f"dW{par}", waits=usew(f"wb{par}"))
            for n in range(4):
                lastmm = k.emit("tensor",
                                lambda e, kc=kc, n=n, par=par: e.matmul(
                                    pu(4 + n)[0:1, :],
                                    COLS["sil"][:, kc:kc + 1],
                                    WB(par)[:, n * 512:(n + 1) * 512],
                                    start=(kc == 0), stop=(kc == NF - 1)),
                                waits=[(f"dW{par}", nd), ("eS", es_sil)] +
                                      (usew(f"pu{4 + n}") if kc == 0 else []),
                                inc="eT" if n == 3 else None)
            setfree(f"wb{par}", "eT", lastmm)
        lasts = None
        for n in range(4):
            lasts = k.emit("scalar",
                           lambda e, n=n: e.copy(STG()[0:1, n * 512:(n + 1) * 512],
                                                 pu(4 + n)[0:1, :]),
                           waits=[("eT", lastmm)], inc="eS")
            setfree(f"pu{4 + n}", "eS", lasts)
        bo = 16 + (g % 2) * 16
        lastr = None
        for t in range(NF):
            lastr = k.emit("tensor",
                           lambda e, t=t, bo=bo: e.matmul(
                               PT[:, bo + t: bo + t + 1],
                               STG()[0:1, t * 128:(t + 1) * 128],
                               ONESR[0:1, 0:1], start=True, stop=True),
                           waits=[("eS", lasts)] + (usew(f"ptg{g % 2}") if t == 0 else []),
                           inc="eT" if t == NF - 1 else None)
        ev = k.emit("vector",
                    lambda e, gname=gname, bo=bo: e.tensor_copy(COLS[gname][:],
                                                                PT[:, bo:bo + NF]),
                    waits=[("eT", lastr)], inc="eV")
        setfree(f"ptg{g % 2}", "eV", ev)
        setfree("stage", "eT", lastr)
    mod_ev = ev

    es = k.emit("scalar", lambda e: e.activation(COLS["t3"][:], COLS["td"][:], AF.Exp),
                inc="eS")
    es = k.emit("scalar", lambda e: e.activation(COLS["w"][:], COLS["t3"][:], AF.Exp,
                                                 scale=-1.0),
                waits=[("eS", es)], inc="eS")
    es_weu = k.emit("scalar", lambda e: e.activation(COLS["eu"][:], COLS["tf"][:],
                                                     AF.Exp), inc="eS")
    k.emit("vector", lambda e: e.tensor_scalar_add(COLS["ca1"][:], COLS["ca"][:], 1.0),
           waits=[("eV", mod_ev)])
    k.emit("vector", lambda e: e.tensor_scalar_add(COLS["cb1"][:], COLS["cb"][:], 1.0))

    def emit_clamp(dst, src):
        t1, t2, t3 = COLS["tmp"], COLS["t2"], COLS["t3"]
        k.emit("vector", lambda e: e.tensor_scalar_max(t1[:], COLS[src][:], 0.25),
               waits=[("eS", es_weu)], inc="eV")
        k.emit("vector", lambda e: e.tensor_scalar_min(t2[:], COLS[src][:], -0.25),
               inc="eV")
        n_ = k.emit("vector", lambda e: e.tensor_single_scalar(t3[:], COLS[src][:],
                                                               0.0, Alu.is_ge),
                    inc="eV")
        k.emit("vector", lambda e: e.tensor_sub(t1[:], t1[:], t2[:]),
               waits=[("eV", n_)])
        k.emit("vector", lambda e: e.tensor_mul(t1[:], t3[:], t1[:]))
        return k.emit("vector", lambda e: e.tensor_add(COLS[dst][:], t2[:], t1[:]),
                      inc="eV")

    emit_clamp("gax", "ga")
    emit_clamp("gbx", "gb")
    k.emit("vector", lambda e: e.tensor_mul(COLS["gag"][:], COLS["gax"][:],
                                            COLS["lng"][:]))
    setup_ev = k.emit("vector", lambda e: e.tensor_mul(COLS["gab"][:], COLS["gax"][:],
                                                       COLS["lnb"][:]), inc="eV")

    def phase_load_x(c):
        lv = None
        for tg in range(4):
            nd = dma_s(STG()[:, :], d_xs[c * TC + tg * 128: c * TC + (tg + 1) * 128, :],
                       sem="dX", waits=usew("stage"))
            lastt = None
            for f in range(NF):
                dstp = PT[:, (f % 4) * 512 + (f // 4) * 128:
                          (f % 4) * 512 + (f // 4) * 128 + 128]
                lastt = k.emit("tensor",
                               lambda e, f=f, d=dstp: e.transpose(
                                   d, STG()[:, f * 128:(f + 1) * 128], IDN[:]),
                               waits=([("dX", nd)] + usew("pt01", "pt23"))
                                     if f == 0 else (),
                               inc="eT" if f == NF - 1 else None)
            lv = None
            for f in range(NF):
                src = PT[:, (f % 4) * 512 + (f // 4) * 128:
                         (f % 4) * 512 + (f // 4) * 128 + 128]
                lv = k.emit("vector",
                            lambda e, s=src, f=f, tg=tg: e.tensor_copy(
                                X(f, tg * 128, 128), s),
                            waits=[("eT", lastt)] if f == 0 else (),
                            inc="eV" if f == NF - 1 else None)
            setfree("stage", "eV", lv)
            setfree("pt01", "eV", lv)
            setfree("pt23", "eV", lv)
        return ("eV", lv)

    def phase_ln(c, src_tile, ready=()):
        ready = list(ready)
        lastt = None
        for f in range(NF):
            par = f % 2
            es_ = k.emit("scalar",
                         lambda e, f=f, par=par: e.activation(SQ(par), src_tile(f),
                                                              AF.Square),
                         waits=usew(f"sq{par}") + (ready if f == 0 else []),
                         inc="eS")
            k.emit("tensor",
                   lambda e, f=f: e.matmul(pu(6)[0:1, :], ONESC[:], src_tile(f),
                                           start=(f == 0), stop=(f == NF - 1)),
                   waits=(usew("pu6", "pu7") + ready) if f == 0 else ())
            lastt = k.emit("tensor",
                           lambda e, f=f, par=par: e.matmul(
                               pu(7)[0:1, :], ONESC[:], SQ(par),
                               start=(f == 0), stop=(f == NF - 1)),
                           waits=[("eS", es_)], inc="eT")
            setfree(f"sq{par}", "eT", lastt)
        ev_ = k.emit("vector",
                     lambda e: e.tensor_scalar_mul(R_mu, pu(6)[0:1, :], 1.0 / C),
                     waits=[("eT", lastt)] + usew("stage"), inc="eV")
        ev_ = k.emit("vector", lambda e: e.tensor_scalar_mul(R_v, pu(7)[0:1, :],
                                                             1.0 / C), inc="eV")
        k.emit("vector", lambda e: e.tensor_mul(R_t, R_mu, R_mu),
               waits=[("eV", ev_)])
        k.emit("vector", lambda e: e.tensor_sub(R_v, R_v, R_t))
        ev_ = k.emit("vector", lambda e: e.tensor_scalar_add(R_v, R_v, EPS),
                     inc="eV")
        setfree("pu6", "eV", ev_)
        setfree("pu7", "eV", ev_)
        es_ = k.emit("scalar", lambda e: e.activation(R_v, R_v, AF.Ln),
                     waits=[("eV", ev_)], inc="eS")
        es_ = k.emit("scalar", lambda e: e.activation(R_v, R_v, AF.Exp, scale=-0.5),
                     waits=[("eS", es_)], inc="eS")
        ev_ = k.emit("vector", lambda e: e.tensor_mul(R_mr, R_mu, R_v),
                     waits=[("eS", es_)], inc="eV")
        k.emit("tensor", lambda e: e.matmul(pt(0), ONESR[:], R_v, start=True,
                                            stop=True),
               waits=[("eV", ev_)] + usew("pt01"))
        lastt = k.emit("tensor", lambda e: e.matmul(pt(1), ONESR[:], R_mr,
                                                    start=True, stop=True), inc="eT")
        setfree("stage", "eT", lastt)
        return ("eT", lastt)

    def phase_mix(c, tm_name, ready=None):
        ev_ = None
        for f in range(NF):
            par = f % 2
            k.emit("vector",
                   lambda e, f=f, par=par: e.tensor_tensor(
                       SQ(par)[:, 1:512], S1(f, 0, 511), S1(f, 1, 511), Alu.subtract),
                   waits=usew(f"sq{par}") + ([ready] if ready and f == 0 else []))
            k.emit("vector",
                   lambda e, f=f, par=par: e.tensor_tensor(
                       SQ(par)[:, 0:1], col("hl", f), S1(f, 0, 1), Alu.subtract))
            k.emit("vector",
                   lambda e, f=f, par=par, tm=tm_name: e.scalar_tensor_tensor(
                       SQ(par), SQ(par), col(tm, f), S1(f), Alu.mult, Alu.add))
            k.emit("vector", lambda e, f=f, par=par: e.tensor_copy(MhT := None or Mh(f), SQ(par)))
            ev_ = k.emit("vector",
                         lambda e, f=f, par=par: e.tensor_tensor(
                             Ml(f), SQ(par), Mh(f), Alu.subtract),
                         inc="eV")
            setfree(f"sq{par}", "eV", ev_)
        return ("eV", ev_)

    def phase_proj(c, wname, consumer, ready, moving_h, moving_l):
        for j in range(NF):
            par = j % 2
            nd = dma_s(STG()[:, :], wp[wname][j], sem="dX", waits=usew("stage"))
            hs = k.emit("scalar",
                        lambda e, par=par: e.copy(WhV(par)[:], STG()[:, :]),
                        waits=[("dX", nd)] + usew(f"wb{par}"), inc="eS")
            lo = k.emit("vector",
                        lambda e, par=par: e.tensor_tensor(WlV(par)[:], STG()[:, :],
                                                           WhV(par)[:], Alu.subtract),
                        waits=[("eS", hs)], inc="eV")
            setfree("stage", "eV", lo)
            lastt = None
            nmm = 0
            for kc in range(NF):
                for wv, mv in ((WhV, moving_h), (WhV, moving_l), (WlV, moving_h)):
                    st_ = nmm == 0
                    sp_ = nmm == 3 * NF - 1
                    nmm += 1
                    lastt = k.emit("tensor",
                                   lambda e, j=j, kc=kc, par=par, wv=wv, mv=mv,
                                          st_=st_, sp_=sp_: e.matmul(
                                       pu(4 + par), wv(par)[:, kc * 128:(kc + 1) * 128],
                                       mv(kc), start=st_, stop=sp_),
                                   waits=([("eV", lo)] + usew(f"pu{4 + par}") +
                                          ([ready] if ready and j == 0 else []))
                                         if st_ else (),
                                   inc="eT" if sp_ else None)
            setfree(f"wb{par}", "eT", lastt)
            sem, cnt = consumer(j, pu(4 + par), lastt)
            setfree(f"pu{4 + par}", sem, cnt)

    def phase_scan(c, full, ek_done):
        ev_ = None
        for j in range(NF):
            wt = SQ(0)
            wt_ev = k.emit("vector",
                           lambda e, j=j, wt=wt: e.tensor_scalar(
                               wt[:], S4(0), ZEROC[:], col("w", j),
                               Alu.mult, Alu.add),
                           waits=usew("sq0") + ([ek_done] if j == 0 else []),
                           inc="eV")
            k.emit("vector", lambda e, j=j: e.tensor_copy(SCA[:, 0:1], col("cA", j)),
                   waits=[("eV", wt_ev)])
            k.emit("vector",
                   lambda e, j=j, wt=wt: e.tensor_tensor_scan(
                       SCA[:, 1:513], wt[:], S1(j), col("cA", j), Alu.mult, Alu.add),
                   inc="eV")
            k.emit("vector", lambda e, j=j: e.tensor_copy(SCB[:, 0:1], col("cB", j)))
            sc_ev = k.emit("vector",
                           lambda e, j=j, wt=wt: e.tensor_tensor_scan(
                               SCB[:, 1:513], wt[:], S4(j), col("cB", j),
                               Alu.mult, Alu.add),
                           inc="eV")
            k.emit("vector", lambda e, j=j: e.tensor_copy(col("cA", j),
                                                          SCA[:, 512:513]),
                   waits=[("eV", sc_ev)])
            ev_ = k.emit("vector",
                         lambda e, j=j: e.tensor_copy(col("cB", j), SCB[:, 512:513]),
                         inc="eV")
            setfree("sq0", "eV", ev_)
            if not full:
                continue
            num_ev = k.emit("vector",
                            lambda e, j=j: e.scalar_tensor_tensor(
                                SCA[:, 0:512], S1(j), col("eu", j), SCA[:, 0:512],
                                Alu.mult, Alu.add), inc="eV")
            ev_ = k.emit("vector",
                         lambda e, j=j: e.scalar_tensor_tensor(
                             SCB[:, 0:512], S4(j), col("eu", j), SCB[:, 0:512],
                             Alu.mult, Alu.add), inc="eV")
            es_ = k.emit("scalar", lambda e: e.activation(SCB[:, 0:512],
                                                          SCB[:, 0:512], AF.Ln),
                         waits=[("eV", ev_)], inc="eS")
            es_ = k.emit("scalar", lambda e: e.activation(SCB[:, 0:512],
                                                          SCB[:, 0:512], AF.Exp,
                                                          scale=-1.0),
                         waits=[("eS", es_)], inc="eS")
            k.emit("vector", lambda e: e.tensor_mul(SCA[:, 0:512], SCA[:, 0:512],
                                                    SCB[:, 0:512]),
                   waits=[("eS", es_), ("eV", num_ev)])
            k.emit("vector", lambda e, j=j: e.tensor_mul(SQ(1), SCA[:, 0:512], S3(j)))
            k.emit("vector", lambda e, j=j: e.tensor_copy(ATTh(j), SQ(1)))
            ev_ = k.emit("vector",
                         lambda e, j=j: e.tensor_tensor(ATTl(j), SQ(1), ATTh(j),
                                                        Alu.subtract),
                         inc="eV")
        if not full:
            k.emit("vector", lambda e: e.tensor_scalar_mul(COLS["cA"][:],
                                                           COLS["cA"][:], CFLAG[:]))
            ev_ = k.emit("vector", lambda e: e.tensor_scalar_mul(COLS["cB"][:],
                                                                 COLS["cB"][:],
                                                                 CFLAG[:]), inc="eV")
        return ("eV", ev_)

    def phase_store(c, x_ready):
        for tg in range(4):
            lastt = None
            for f in range(NF):
                dstp = PT[:, f * 128: f * 128 + 128]
                lastt = k.emit("tensor",
                               lambda e, f=f, tg=tg, d=dstp: e.transpose(
                                   d, X(f, tg * 128, 128), IDN[:]),
                               waits=(usew("pt01", "pt23") +
                                      ([x_ready] if tg == 0 else []))
                                     if f == 0 else (),
                               inc="eT" if f == NF - 1 else None)
            ev_ = k.emit("vector",
                         lambda e: e.tensor_copy(STG()[:, :], PT[:, 0:2048]),
                         waits=[("eT", lastt)] + usew("stage"), inc="eV")
            setfree("pt01", "eV", ev_)
            setfree("pt23", "eV", ev_)
            nd = k.emit("gpsimd",
                        lambda e, c=c, tg=tg: e.dma_start(
                            out=d_y[(c - 1) * TC + tg * 128:
                                    (c - 1) * TC + (tg + 1) * 128, :],
                            in_=STG()[:, :]),
                        waits=[("eV", ev_)], inc="dG", amt=16)
            setfree("stage", "dG", nd)

    def ln_apply(c, src_tile, bc_ready, emit_f):
        ev_ = None
        for f in range(NF):
            par = f % 2
            k.emit("vector",
                   lambda e, f=f, par=par: e.tensor_tensor(SQ(par), src_tile(f),
                                                           pt(0), Alu.mult),
                   waits=([bc_ready] if f == 0 else []) + usew(f"sq{par}"))
            k.emit("vector",
                   lambda e, f=f, par=par: e.tensor_tensor(SQ(par), SQ(par), pt(1),
                                                           Alu.subtract))
            ev_ = emit_f(f, par)
            setfree(f"sq{par}", "eV", ev_)
        setfree("pt01", "eV", ev_)
        return ("eV", ev_)

    def dbg_dump(off):
        nd = k.emit("gpsimd",
               lambda e: e.dma_start(out=d_dbg[:, :], in_=AR[:, off:off + 8192]),
               waits=[("eV", k.now("eV")), ("eS", k.now("eS")),
                      ("eT", k.now("eT"))], inc="dG", amt=16)
        for eng_ in ("vector", "scalar", "tensor", "sync"):
            k.emit(eng_, lambda e: None, waits=[("dG", nd)], noinc=True)

    def dbg_dump_cols(names):
        for i, nm in enumerate(names):
            k.emit("gpsimd",
                   lambda e, i=i, nm=nm: e.dma_start(
                       out=d_dbg[:, i * 16:(i + 1) * 16], in_=COLS[nm][:]),
                   waits=[("eV", k.now("eV")), ("eS", k.now("eS"))],
                   inc="dG", amt=16)

    if DBG == ("cols", -1):
        dbg_dump_cols(("sa", "ca", "gax", "sb", "cb", "gbx", "w", "eu", "sil",
                       "ca1", "cb1", "gag", "gab"))

    ek_done = [None]
    for c in range(NCHUNK):
        full = c > 0
        x_loaded = phase_load_x(c)

        bc = phase_ln(c, lambda f: X(f),
                      ready=[x_loaded] + ([("eV", setup_ev)] if c == 0 else []))
        if DBG == ("x", c):
            dbg_dump(OFF_X)
        h_ev = ln_apply(c, lambda f: X(f), bc,
                        lambda f, par: k.emit("vector",
                                              lambda e, f=f, par=par: e.tensor_scalar(
                                                  S1(f), SQ(par), col("ca1", f),
                                                  col("sa", f), Alu.mult, Alu.add),
                                              inc="eV"))

        def cons_r(j, bank, tcnt):
            cnt = k.emit("scalar",
                         lambda e, j=j, bank=bank: e.activation(S3(j), bank,
                                                                AF.Sigmoid),
                         waits=[("eT", tcnt)], inc="eS")
            return ("eS", cnt)

        def cons_g(j, bank, tcnt):
            cs = k.emit("scalar",
                        lambda e, bank=bank: e.activation(GT[:], bank, AF.Sigmoid),
                        waits=[("eT", tcnt)] + usew("gt"), inc="eS")
            cv = k.emit("vector", lambda e, j=j: e.tensor_mul(S3(j), S3(j), GT[:]),
                        waits=[("eS", cs)], inc="eV")
            setfree("gt", "eV", cv)
            return ("eS", cs)

        def cons_k(j, bank, tcnt):
            cnt = k.emit("scalar",
                         lambda e, j=j, bank=bank: e.activation(S4(j), bank, AF.Exp),
                         waits=[("eT", tcnt)], inc="eS")
            return ("eS", cnt)

        def cons_v(j, bank, tcnt):
            cnt = k.emit("vector",
                         lambda e, j=j, bank=bank: e.tensor_tensor(S1(j), S4(j),
                                                                   bank, Alu.mult),
                         waits=[("eT", tcnt)] + ([ek_done[0]] if j == 0 else []),
                         inc="eV")
            return ("eV", cnt)

        if DBG == ("h", c):
            dbg_dump(OFF_S1)
        if full:
            mev = phase_mix(c, "tmr", ready=h_ev)
            phase_proj(c, "wr", cons_r, mev, Mh, Ml)
            mev = phase_mix(c, "tmg", ready=("eT", k.now("eT")))
            phase_proj(c, "wg", cons_g, mev, Mh, Ml)
        mev = phase_mix(c, "tmk",
                        ready=("eT", k.now("eT")) if full else h_ev)
        if DBG == ("mixk", c):
            dbg_dump(OFF_S2)
        phase_proj(c, "wk", cons_k, mev, Mh, Ml)
        ek_done[0] = ("eS", k.now("eS"))
        if DBG == ("ek", c):
            dbg_dump(OFF_S4)
        mev = phase_mix(c, "tmv", ready=("eT", k.now("eT")))
        k.emit("vector", lambda e: e.tensor_copy(
            COLS["hl"][:], AR[:, OFF_S1 + 511: OFF_S1 + 8192: 512]))
        phase_proj(c, "wv", cons_v, mev, Mh, Ml)

        if DBG == ("ekv", c):
            dbg_dump(OFF_S1)
        if DBG == ("sg", c):
            dbg_dump(OFF_S3)
        scan_ev = phase_scan(c, full, ek_done[0])
        if DBG == ("carry", c):
            for i_, nm_ in ((0, "cA"), (1, "cB")):
                k.emit("gpsimd",
                       lambda e, i_=i_, nm_=nm_: e.dma_start(
                           out=d_dbg[:, i_ * 16:(i_ + 1) * 16], in_=COLS[nm_][:]),
                       waits=[("eV", k.now("eV"))], inc="dG", amt=16)
            k.emit("vector", lambda e: None, waits=[("dG", k.now("dG"))])
        if not full:
            continue

        if DBG == ("att", c):
            dbg_dump(OFF_S1)
        o_cnt = [0]

        def cons_o(j, bank, tcnt):
            cnt = k.emit("scalar", lambda e, j=j, bank=bank: e.copy(S2(j), bank),
                         waits=[("eT", tcnt)], inc="eS")
            o_cnt[0] = cnt
            return ("eS", cnt)

        phase_proj(c, "wo", cons_o, scan_ev, ATTh, ATTl)

        if DBG == ("obuf", c):
            dbg_dump(OFF_S2)
        bc = phase_ln(c, lambda f: S2(f), ready=[("eS", o_cnt[0])])
        if DBG == ("rowso", c):
            nd_ = k.emit("gpsimd",
                         lambda e: e.dma_start(out=d_dbg[:, 0:2048],
                                               in_=AR[:, OFF_STAGE:OFF_STAGE + 2048]),
                         waits=[("eV", k.now("eV")), ("eS", k.now("eS")),
                                ("eT", k.now("eT"))], inc="dG", amt=16)
            for eng_ in ("vector", "scalar", "tensor", "sync"):
                k.emit(eng_, lambda e: None, waits=[("dG", nd_)], noinc=True)
        if DBG == ("xpre", c):
            dbg_dump(OFF_X)

        def fin_o(f, par):
            n_ = k.emit("vector",
                        lambda e, f=f, par=par: e.tensor_scalar(
                            SQ(par), SQ(par), col("gag", f), col("gab", f),
                            Alu.mult, Alu.add), inc="eV")
            return k.emit("vector",
                          lambda e, f=f, par=par: e.tensor_tensor(X(f), X(f),
                                                                  SQ(par), Alu.add),
                          waits=[("eV", n_)], inc="eV")

        x_attn = ln_apply(c, lambda f: S2(f), bc, fin_o)
        if DBG == ("bco", c):
            nv_ = k.emit("vector", lambda e: e.tensor_copy(
                AR[:, OFF_STAGE:OFF_STAGE + 512], pt(0)), inc="eV")
            nv_ = k.emit("vector", lambda e: e.tensor_copy(
                AR[:, OFF_STAGE + 512:OFF_STAGE + 1024], pt(1)), inc="eV")
            nd_ = k.emit("gpsimd",
                         lambda e: e.dma_start(out=d_dbg[:, 0:2048],
                                               in_=AR[:, OFF_STAGE:OFF_STAGE + 2048]),
                         waits=[("eV", nv_)], inc="dG", amt=16)
            for eng_ in ("vector", "scalar", "tensor", "sync"):
                k.emit(eng_, lambda e: None, waits=[("dG", nd_)], noinc=True)
        if DBG == ("obuf2", c):
            dbg_dump(OFF_S2)
        if DBG == ("rowso2", c):
            nd_ = k.emit("gpsimd",
                         lambda e: e.dma_start(out=d_dbg[:, 0:2048],
                                               in_=AR[:, OFF_STAGE:OFF_STAGE + 2048]),
                         waits=[("eV", k.now("eV")), ("eS", k.now("eS")),
                                ("eT", k.now("eT"))], inc="dG", amt=16)
            for eng_ in ("vector", "scalar", "tensor", "sync"):
                k.emit(eng_, lambda e: None, waits=[("dG", nd_)], noinc=True)

        if DBG == ("xattn", c):
            dbg_dump(OFF_X)
        bc = phase_ln(c, lambda f: X(f), ready=[x_attn])

        def fin_h2(f, par):
            return k.emit("vector",
                          lambda e, f=f, par=par: e.tensor_scalar(
                              H2F(f), SQ(par), col("cb1", f), col("sb", f),
                              Alu.mult, Alu.add),
                          inc="eV")

        h2_ev = ln_apply(c, lambda f: X(f), bc, fin_h2)

        if DBG == ("h2", c):
            dbg_dump(OFF_S4)
        for nm in ("wland0", "wland1", "whi0", "whi1", "wlo0", "wlo1"):
            freed[nm] = x_attn
        last_cs = None
        for m in range(NH):
            par = m % 2
            nd = dma_s(WLAND(par)[:], wp["w0"][m], sem=f"dW{par}", waits=usew(f"wland{par}"))
            cs = k.emit("scalar",
                        lambda e, par=par: e.copy(WHI(par)[:], WLAND(par)[:]),
                        waits=[(f"dW{par}", nd)] + usew(f"whi{par}"), inc="eS")
            cv = k.emit("vector",
                        lambda e, par=par: e.tensor_tensor(WLO(par)[:], WLAND(par)[:],
                                                           WHI(par)[:], Alu.subtract),
                        waits=[("eS", cs)] + usew(f"wlo{par}"), inc="eV")
            setfree(f"wland{par}", "eV", cv)
            lastt = None
            for p_, wreg in ((0, WHI), (1, WLO)):
                for kc in range(NF):
                    st_ = (p_ == 0 and kc == 0)
                    sp_ = (p_ == 1 and kc == NF - 1)
                    lastt = k.emit("tensor",
                                   lambda e, kc=kc, par=par, wreg=wreg, st_=st_,
                                          sp_=sp_: e.matmul(
                                       pu(4 + par), wreg(par)[:, kc * 128:(kc + 1) * 128],
                                       H2F(kc), start=st_, stop=sp_),
                                   waits=([("eV", cv)] + usew(f"pu{4 + par}") +
                                          ([h2_ev] if m == 0 else []))
                                         if st_ else (),
                                   inc="eT" if sp_ else None)
            setfree(f"whi{par}", "eT", lastt)
            setfree(f"wlo{par}", "eT", lastt)
            last_cs = k.emit("scalar",
                             lambda e, m=m, par=par: e.activation(
                                 DF(m), pu(4 + par), AF.Gelu, bias=B0C[:, m:m + 1]),
                             waits=[("eT", lastt)], inc="eS")
            setfree(f"pu{4 + par}", "eS", last_cs)
        d_ready = ("eS", last_cs)

        if DBG == ("dff", c):
            dbg_dump(OFF_S1)
        if DBG == ("dff2", c):
            dbg_dump(OFF_S3)
        freed["w1land"] = freed.get("wland0", ("eV", 0))
        for j in range(NF):
            parj = j % 2
            lastt = None
            for qt in range(4):
                qpar = qt % 2
                nd = dma_s(ar(OFF_S2, 2048),
                           wp["w1"][j][:, qt * 2048:(qt + 1) * 2048],
                           sem="dQ", waits=usew("w1land", "wland1"))
                freed.pop("wland1", None)
                cs = k.emit("scalar",
                            lambda e, qpar=qpar: e.copy(WHI(qpar)[:], ar(OFF_S2, 2048)),
                            waits=[("dQ", nd)] + usew(f"whi{qpar}"), inc="eS")
                cv = k.emit("vector",
                            lambda e, qpar=qpar: e.tensor_tensor(
                                WLO(qpar)[:], ar(OFF_S2, 2048), WHI(qpar)[:],
                                Alu.subtract),
                            waits=[("eS", cs)] + usew(f"wlo{qpar}"), inc="eV")
                setfree("w1land", "eV", cv)
                for p_, wreg in ((0, WHI), (1, WLO)):
                    for kc in range(NF):
                        m = qt * NF + kc
                        st_ = (qt == 0 and p_ == 0 and kc == 0)
                        sp_ = (qt == 3 and p_ == 1 and kc == NF - 1)
                        lastt = k.emit("tensor",
                                       lambda e, m=m, kc=kc, qpar=qpar,
                                              wreg=wreg, parj=parj, st_=st_,
                                              sp_=sp_: e.matmul(
                                           pu(6 + parj),
                                           wreg(qpar)[:, kc * 128:(kc + 1) * 128],
                                           DF(m), start=st_, stop=sp_),
                                       waits=([("eV", cv)] +
                                              (usew(f"pu{6 + parj}") +
                                               ([d_ready] if j == 0 else [])
                                               if qt == 0 else []))
                                             if p_ == 0 and kc == 0 else (),
                                       inc="eT" if sp_ else None)
                setfree(f"whi{qpar}", "eT", lastt)
                setfree(f"wlo{qpar}", "eT", lastt)
            cs = k.emit("vector",
                        lambda e, j=j, parj=parj: e.tensor_single_scalar(
                            E16[:], pu(6 + parj), col("b1", j), Alu.add),
                        waits=[("eT", lastt)] + usew("e16"), inc="eV")
            setfree(f"pu{6 + parj}", "eV", cs)
            cv = k.emit("vector",
                        lambda e, j=j: e.scalar_tensor_tensor(
                            X(j), E16[:], col("gbx", j), X(j), Alu.mult, Alu.add),
                        waits=[("eV", cs)], inc="eV")
            setfree("e16", "eV", cv)
        x_final = ("eV", cv)

        if DBG == ("xfin", c):
            dbg_dump(OFF_X)
        phase_store(c, x_final)

    k.emit("gpsimd", lambda e: None, waits=[("dG", k.now("dG"))], noinc=True)
    k.emit("sync", lambda e: None, waits=[("dG", k.now("dG"))], noinc=True)

    with nc.Block() as block:
        @block.sync
        def _(eng):
            k.replay("sync", eng, sems)

        @block.gpsimd
        def _(eng):
            k.replay("gpsimd", eng, sems)

        @block.tensor
        def _(eng):
            k.replay("tensor", eng, sems)

        @block.vector
        def _(eng):
            k.replay("vector", eng, sems)

        @block.scalar
        def _(eng):
            k.replay("scalar", eng, sems)

    st.close()
    return nc


def _pack_w(wt):
    K, J = wt.shape
    return np.ascontiguousarray(
        wt.reshape(K // 128, 128, J // 128, 128).transpose(2, 1, 0, 3)
        .reshape(J // 128, 128, K))


def _cols(v):
    return np.ascontiguousarray(np.asarray(v, dtype=np.float32).reshape(-1, 128).T)


def kernel(L0_attn_x, L0_attn_time_decay, L0_attn_time_first, L0_attn_tm_k,
           L0_attn_tm_v, L0_attn_tm_r, L0_attn_tm_g, L0_attn_Wr, L0_attn_Wk,
           L0_attn_Wv, L0_attn_Wg, L0_attn_Wo, L0_attn_ln_g, L0_attn_ln_b,
           L0_ffn_W0, L0_ffn_bias0, L0_ffn_W1, L0_ffn_bias1, L0_c, L0_wmod):
    x = np.asarray(L0_attn_x, dtype=np.float32)
    shared = {
        "wr": _pack_w(np.ascontiguousarray(np.asarray(L0_attn_Wr, dtype=np.float32).T)),
        "wk": _pack_w(np.ascontiguousarray(np.asarray(L0_attn_Wk, dtype=np.float32).T)),
        "wv": _pack_w(np.ascontiguousarray(np.asarray(L0_attn_Wv, dtype=np.float32).T)),
        "wg": _pack_w(np.ascontiguousarray(np.asarray(L0_attn_Wg, dtype=np.float32).T)),
        "wo": _pack_w(np.ascontiguousarray(np.asarray(L0_attn_Wo, dtype=np.float32).T)),
        "w0": _pack_w(np.ascontiguousarray(np.asarray(L0_ffn_W0, dtype=np.float32).T)),
        "w1": _pack_w(np.ascontiguousarray(np.asarray(L0_ffn_W1, dtype=np.float32).T)),
        "wmod": np.ascontiguousarray(np.asarray(L0_wmod, dtype=np.float32)),
        "td": _cols(L0_attn_time_decay),
        "tf": _cols(L0_attn_time_first),
        "tmk": _cols(L0_attn_tm_k),
        "tmv": _cols(L0_attn_tm_v),
        "tmr": _cols(L0_attn_tm_r),
        "tmg": _cols(L0_attn_tm_g),
        "lng": _cols(L0_attn_ln_g),
        "lnb": _cols(L0_attn_ln_b),
        "b1": _cols(L0_ffn_bias1),
        "b0": _cols(L0_ffn_bias0),
        "idn": np.eye(128, dtype=np.float32),
    }
    in_maps = []
    for core in range(NCORES):
        b, q = divmod(core, 4)
        xs = np.zeros((ROWS, C), dtype=np.float32)
        if q == 0:
            xs[511] = x[b, 1]
            xs[512:] = x[b, 0:2048]
            cflag = np.zeros((128, 1), dtype=np.float32)
        else:
            xs[:] = x[b, q * 2048 - 512: q * 2048 + 2048]
            cflag = np.ones((128, 1), dtype=np.float32)
        im = dict(shared)
        im["xs"] = xs
        im["cflag"] = cflag
        im["crow"] = np.ascontiguousarray(
            np.asarray(L0_c, dtype=np.float32)[b:b + 1])
        in_maps.append(im)

    nc = build_program()
    res = run_bass_kernel_spmd(nc, in_maps, core_ids=list(range(NCORES)))
    out = np.empty((BATCH, T, C), dtype=np.float32)
    for core in range(NCORES):
        b, q = divmod(core, 4)
        out[b, q * 2048:(q + 1) * 2048] = res.results[core]["y"]
    return out


# version 05 (final)
# speedup vs baseline: 1.2489x; 1.2489x over previous
import numpy as np
from contextlib import ExitStack

import concourse.bass as bass
import concourse.mybir as mybir
from concourse.bass_utils import run_bass_kernel_spmd

F32 = mybir.dt.float32
F16 = mybir.dt.float16
BF16 = mybir.dt.bfloat16
AF = mybir.ActivationFunctionType
Alu = mybir.AluOpType

C = 2048
T = 8192
BATCH = 2
NCORES = 8
TC = 512
NF = C // 128
NCHUNK = 5
ROWS = NCHUNK * TC
HID = 8192
NH = HID // 128
EPS = 1e-5

ENGINES = ("sync", "gpsimd", "tensor", "vector", "scalar")

DBG = None

OFF_X = 0
OFF_S1 = 8192
OFF_S2 = 16384
OFF_S3 = 24576
OFF_S4 = 32768
OFF_WB = 40960
OFF_A = 45056
OFF_B = 45569
OFF_STAGE = 46082
OFF_SQ = 48130
ARENA = 49154


class Sched:
    def __init__(self):
        self.ops = {e: [] for e in ENGINES}
        self.cnt = {}

    STRICT = {"vector": "eV", "scalar": "eS"}

    def emit(self, eng, fn, waits=(), inc=None, amt=1, noinc=False):
        ws = [(s, v) for (s, v) in waits if v and v > 0]
        own = self.STRICT.get(eng)
        if own is not None and not noinc:
            prev = self.cnt.get(own, 0)
            if prev > 0:
                ws.append((own, prev))
            if inc is None:
                inc, amt = own, 1
            else:
                assert inc == own, (eng, inc)
        self.ops[eng].append((fn, tuple(ws), inc, amt))
        if inc is not None:
            self.cnt[inc] = self.cnt.get(inc, 0) + amt
            return self.cnt[inc]
        return None

    def now(self, sem):
        return self.cnt.get(sem, 0)

    def replay(self, engname, eng, sems):
        waited = {}
        for fn, waits, inc, amt in self.ops[engname]:
            for s, v in waits:
                if waited.get(s, 0) >= v:
                    continue
                eng.wait_ge(sems[s], v)
                waited[s] = v
            ins = fn(eng)
            if inc is not None:
                ins.then_inc(sems[inc], amt)


def build_program():
    import os as _os
    nc = bass.Bass(trn_type="TRN2", target_bir_lowering=False,
                   detect_race_conditions=bool(_os.environ.get("KDETECT")))

    d_xs = nc.dram_tensor("xs", [ROWS, C], F32, kind="ExternalInput").ap()
    d_y = nc.dram_tensor("y", [4 * TC, C], F32, kind="ExternalOutput").ap()
    wp = {}
    for nm in ("wr", "wk", "wv", "wg", "wo"):
        wp[nm] = nc.dram_tensor(nm, [NF, 128, C], F32, kind="ExternalInput").ap()
    wp["w0"] = nc.dram_tensor("w0", [NH, 128, C], F32, kind="ExternalInput").ap()
    wp["w1"] = nc.dram_tensor("w1", [NF, 128, HID], F32, kind="ExternalInput").ap()
    d_wmod = nc.dram_tensor("wmod", [C, 6 * C], F32, kind="ExternalInput").ap()
    d_crow = nc.dram_tensor("crow", [1, C], F32, kind="ExternalInput").ap()
    d_cols = {}
    for nm in ("td", "tf", "tmk", "tmv", "tmr", "tmg", "lng", "lnb", "b1"):
        d_cols[nm] = nc.dram_tensor(nm, [128, NF], F32, kind="ExternalInput").ap()
    d_b0 = nc.dram_tensor("b0", [128, NH], F32, kind="ExternalInput").ap()
    d_cflag = nc.dram_tensor("cflag", [128, 1], F32, kind="ExternalInput").ap()
    d_idn = nc.dram_tensor("idn", [128, 128], F32, kind="ExternalInput").ap()
    d_dbg = nc.dram_tensor("dbg", [128, 8192], F32, kind="ExternalOutput").ap()
    sw = {}
    for nm in ("wr", "wk", "wv", "wg", "wo"):
        sw[nm] = nc.dram_tensor("sw_" + nm, [NF, 128, 4096], BF16).ap()
    sw["w0"] = nc.dram_tensor("sw_w0", [NH, 128, 4096], F16).ap()
    sw["w1"] = nc.dram_tensor("sw_w1", [NF, 4, 128, 4096], F16).ap()

    st = ExitStack()
    sb = lambda name, shape, dt=F32: st.enter_context(nc.sbuf_tensor(name, shape, dt))

    AR = sb("AR", [128, ARENA])
    IDN = sb("IDN", [128, 128])
    ONESC = sb("ONESC", [128, 1])
    ZEROC = sb("ZEROC", [128, 1])
    ONESR = sb("ONESR", [1, 128])
    COLS = {nm: sb("c_" + nm, [128, NF]) for nm in
            ("td", "tf", "tmk", "tmv", "tmr", "tmg", "lng", "lnb", "b1",
             "w", "eu", "sil", "tmp", "t2", "t3", "sa", "ca", "ga", "sb",
             "cb", "gb", "ca1", "cb1", "gax", "gbx", "gag", "gab",
             "cA", "cB", "hl")}
    B0C = sb("B0C", [128, NH])
    CFLAG = sb("CFLAG", [128, 1])
    GT = sb("GT", [128, 512])
    E16 = sb("E16", [128, 512], F16)

    PT = st.enter_context(nc.psum_tensor("PT", [128, 2048], F32))
    PU = st.enter_context(nc.psum_tensor("PU", [128, 2048], F32))

    sems = {s: st.enter_context(nc.semaphore(s))
            for s in ("dC", "dW0", "dW1", "dX", "dQ", "dG", "eT", "eV", "eS",
                      "dP0", "dP1", "dA0", "dA1", "dF0", "dF1", "dQ0", "dQ1")}

    k = Sched()
    freed = {}

    def usew(*names):
        return [freed[n] for n in names if n in freed]

    def setfree(name, sem, cnt):
        freed[name] = (sem, cnt)

    def ar(off, n, p0=0, np_=128):
        return AR[p0:p0 + np_, off:off + n]

    X = lambda f, a=0, n=512: ar(OFF_X + f * 512 + a, n)
    S1 = lambda f, a=0, n=512: ar(OFF_S1 + f * 512 + a, n)
    S2 = lambda f, a=0, n=512: ar(OFF_S2 + f * 512 + a, n)
    S3 = lambda f, a=0, n=512: ar(OFF_S3 + f * 512 + a, n)
    S4 = lambda f, a=0, n=512: ar(OFF_S4 + f * 512 + a, n)
    WB = lambda par: ar(OFF_WB + par * 2048, 2048)
    SQ = lambda par: ar(OFF_SQ + par * 512, 512)
    STG = lambda: ar(OFF_STAGE, 2048)
    R_mu = ar(OFF_STAGE + 0, 512, 0, 1)
    R_v = ar(OFF_STAGE + 512, 512, 0, 1)
    R_t = ar(OFF_STAGE + 1024, 512, 0, 1)
    R_mr = ar(OFF_STAGE + 1536, 512, 0, 1)
    SCA = AR[:, OFF_A:OFF_A + 513]
    SCB = AR[:, OFF_B:OFF_B + 513]
    H2F = lambda f: ar(OFF_S4 + f * 256, 256).bitcast(F16)
    DF = lambda m: ar((OFF_S1 if m < 32 else OFF_S3 - 32 * 256) + m * 256,
                      256).bitcast(F16)
    WhV = lambda par: ar(OFF_WB + par * 2048, 1024).bitcast(BF16)
    WlV = lambda par: ar(OFF_WB + par * 2048 + 1024, 1024).bitcast(BF16)
    Mh = lambda f: ar(OFF_S2 + f * 256, 256).bitcast(BF16)
    Ml = lambda f: ar(OFF_S2 + 4096 + f * 256, 256).bitcast(BF16)
    ATTh = lambda f: ar(OFF_S1 + f * 512, 256).bitcast(BF16)
    ATTl = lambda f: ar(OFF_S1 + f * 512 + 256, 256).bitcast(BF16)
    FWH = lambda par: ar(OFF_S2 + par * 2048, 1024).bitcast(F16)
    FWL = lambda par: ar(OFF_S2 + par * 2048 + 1024, 1024).bitcast(F16)

    col = lambda nm, f: COLS[nm][:, f:f + 1]
    pt = lambda b, a=0, n=512: PT[:, b * 512 + a: b * 512 + a + n]
    pu = lambda b, a=0, n=512: PU[:, (b - 4) * 512 + a: (b - 4) * 512 + a + n]

    def dma_s(out, in_, sem="dC", waits=()):
        return k.emit("sync", lambda e, o=out, i=in_: e.dma_start(out=o, in_=i),
                      waits=waits, inc=sem, amt=16)

    for nm in ("td", "tf", "tmk", "tmv", "tmr", "tmg", "lng", "lnb", "b1"):
        dma_s(COLS[nm][:], d_cols[nm])
    dma_s(B0C[:], d_b0)
    dma_s(CFLAG[:], d_cflag)
    dma_s(IDN[:], d_idn)
    consts_dma = dma_s(STG()[0:1, :], d_crow)

    k.emit("vector", lambda e: e.memset(ONESC[:], 1.0))
    k.emit("vector", lambda e: e.memzero(ZEROC[:]))
    k.emit("vector", lambda e: e.memzero(COLS["cA"][:]))
    k.emit("vector", lambda e: e.memzero(COLS["cB"][:]))
    k.emit("vector", lambda e: e.memzero(COLS["hl"][:]), inc="eV")

    ev0 = k.emit("vector", lambda e: e.memset(ONESR[:], 1.0), inc="eV")
    lastt = None
    for t in range(NF):
        lastt = k.emit("tensor",
                       lambda e, t=t: e.matmul(PT[:, t:t + 1],
                                               STG()[0:1, t * 128:(t + 1) * 128],
                                               ONESR[0:1, 0:1],
                                               start=True, stop=True),
                       waits=[("dC", consts_dma), ("eV", ev0)] if t == 0 else (),
                       inc="eT" if t == NF - 1 else None)
    ev = k.emit("vector", lambda e: e.tensor_copy(COLS["tmp"][:], PT[:, 0:NF]),
                waits=[("eT", lastt)], inc="eV")
    es_sil = k.emit("scalar", lambda e: e.activation(COLS["sil"][:], COLS["tmp"][:],
                                                     AF.Silu),
                    waits=[("eV", ev)], inc="eS")

    for g, gname in enumerate(("sa", "ca", "ga", "sb", "cb", "gb")):
        lastmm = None
        for kc in range(NF):
            par = kc % 2
            nd = dma_s(WB(par)[:], d_wmod[kc * 128:(kc + 1) * 128, g * C:(g + 1) * C],
                       sem=f"dW{par}", waits=usew(f"wb{par}"))
            for n in range(4):
                lastmm = k.emit("tensor",
                                lambda e, kc=kc, n=n, par=par: e.matmul(
                                    pu(4 + n)[0:1, :],
                                    COLS["sil"][:, kc:kc + 1],
                                    WB(par)[:, n * 512:(n + 1) * 512],
                                    start=(kc == 0), stop=(kc == NF - 1)),
                                waits=[(f"dW{par}", nd), ("eS", es_sil)] +
                                      (usew(f"pu{4 + n}") if kc == 0 else []),
                                inc="eT" if n == 3 else None)
            setfree(f"wb{par}", "eT", lastmm)
        lasts = None
        for n in range(4):
            lasts = k.emit("scalar",
                           lambda e, n=n: e.copy(STG()[0:1, n * 512:(n + 1) * 512],
                                                 pu(4 + n)[0:1, :]),
                           waits=[("eT", lastmm)], inc="eS")
            setfree(f"pu{4 + n}", "eS", lasts)
        bo = 16 + (g % 2) * 16
        lastr = None
        for t in range(NF):
            lastr = k.emit("tensor",
                           lambda e, t=t, bo=bo: e.matmul(
                               PT[:, bo + t: bo + t + 1],
                               STG()[0:1, t * 128:(t + 1) * 128],
                               ONESR[0:1, 0:1], start=True, stop=True),
                           waits=[("eS", lasts)] + (usew(f"ptg{g % 2}") if t == 0 else []),
                           inc="eT" if t == NF - 1 else None)
        ev = k.emit("vector",
                    lambda e, gname=gname, bo=bo: e.tensor_copy(COLS[gname][:],
                                                                PT[:, bo:bo + NF]),
                    waits=[("eT", lastr)], inc="eV")
        setfree(f"ptg{g % 2}", "eV", ev)
        setfree("stage", "eT", lastr)
    mod_ev = ev

    es = k.emit("scalar", lambda e: e.activation(COLS["t3"][:], COLS["td"][:], AF.Exp),
                inc="eS")
    es = k.emit("scalar", lambda e: e.activation(COLS["w"][:], COLS["t3"][:], AF.Exp,
                                                 scale=-1.0),
                waits=[("eS", es)], inc="eS")
    es_weu = k.emit("scalar", lambda e: e.activation(COLS["eu"][:], COLS["tf"][:],
                                                     AF.Exp), inc="eS")
    k.emit("vector", lambda e: e.tensor_scalar_add(COLS["ca1"][:], COLS["ca"][:], 1.0),
           waits=[("eV", mod_ev)])
    k.emit("vector", lambda e: e.tensor_scalar_add(COLS["cb1"][:], COLS["cb"][:], 1.0))

    def emit_clamp(dst, src):
        t1, t2, t3 = COLS["tmp"], COLS["t2"], COLS["t3"]
        k.emit("vector", lambda e: e.tensor_scalar_max(t1[:], COLS[src][:], 0.25),
               waits=[("eS", es_weu)], inc="eV")
        k.emit("vector", lambda e: e.tensor_scalar_min(t2[:], COLS[src][:], -0.25),
               inc="eV")
        n_ = k.emit("vector", lambda e: e.tensor_single_scalar(t3[:], COLS[src][:],
                                                               0.0, Alu.is_ge),
                    inc="eV")
        k.emit("vector", lambda e: e.tensor_sub(t1[:], t1[:], t2[:]),
               waits=[("eV", n_)])
        k.emit("vector", lambda e: e.tensor_mul(t1[:], t3[:], t1[:]))
        return k.emit("vector", lambda e: e.tensor_add(COLS[dst][:], t2[:], t1[:]),
                      inc="eV")

    emit_clamp("gax", "ga")
    emit_clamp("gbx", "gb")
    k.emit("vector", lambda e: e.tensor_mul(COLS["gag"][:], COLS["gax"][:],
                                            COLS["lng"][:]))
    setup_ev = k.emit("vector", lambda e: e.tensor_mul(COLS["gab"][:], COLS["gax"][:],
                                                       COLS["lnb"][:]), inc="eV")

    def phase_load_x(c):
        lv = None
        for tg in range(4):
            nd = dma_s(STG()[:, :], d_xs[c * TC + tg * 128: c * TC + (tg + 1) * 128, :],
                       sem="dX", waits=usew("stage"))
            lastt = None
            for f in range(NF):
                dstp = PT[:, (f % 4) * 512 + (f // 4) * 128:
                          (f % 4) * 512 + (f // 4) * 128 + 128]
                lastt = k.emit("tensor",
                               lambda e, f=f, d=dstp: e.transpose(
                                   d, STG()[:, f * 128:(f + 1) * 128], IDN[:]),
                               waits=([("dX", nd)] + usew("pt01", "pt23"))
                                     if f == 0 else (),
                               inc="eT" if f == NF - 1 else None)
            lv = None
            for f in range(NF):
                src = PT[:, (f % 4) * 512 + (f // 4) * 128:
                         (f % 4) * 512 + (f // 4) * 128 + 128]
                lv = k.emit("vector",
                            lambda e, s=src, f=f, tg=tg: e.tensor_copy(
                                X(f, tg * 128, 128), s),
                            waits=[("eT", lastt)] if f == 0 else (),
                            inc="eV" if f == NF - 1 else None)
            setfree("stage", "eV", lv)
            setfree("pt01", "eV", lv)
            setfree("pt23", "eV", lv)
        return ("eV", lv)

    def phase_ln(c, src_tile, ready=()):
        ready = list(ready)
        lastt = None
        for f in range(NF):
            par = f % 2
            es_ = k.emit("scalar",
                         lambda e, f=f, par=par: e.activation(SQ(par), src_tile(f),
                                                              AF.Square),
                         waits=usew(f"sq{par}") + (ready if f == 0 else []),
                         inc="eS")
            k.emit("tensor",
                   lambda e, f=f: e.matmul(pu(6)[0:1, :], ONESC[:], src_tile(f),
                                           start=(f == 0), stop=(f == NF - 1)),
                   waits=(usew("pu6", "pu7") + ready) if f == 0 else ())
            lastt = k.emit("tensor",
                           lambda e, f=f, par=par: e.matmul(
                               pu(7)[0:1, :], ONESC[:], SQ(par),
                               start=(f == 0), stop=(f == NF - 1)),
                           waits=[("eS", es_)], inc="eT")
            setfree(f"sq{par}", "eT", lastt)
        ev_ = k.emit("vector",
                     lambda e: e.tensor_scalar_mul(R_mu, pu(6)[0:1, :], 1.0 / C),
                     waits=[("eT", lastt)] + usew("stage"), inc="eV")
        ev_ = k.emit("vector", lambda e: e.tensor_scalar_mul(R_v, pu(7)[0:1, :],
                                                             1.0 / C), inc="eV")
        k.emit("vector", lambda e: e.tensor_mul(R_t, R_mu, R_mu),
               waits=[("eV", ev_)])
        k.emit("vector", lambda e: e.tensor_sub(R_v, R_v, R_t))
        ev_ = k.emit("vector", lambda e: e.tensor_scalar_add(R_v, R_v, EPS),
                     inc="eV")
        setfree("pu6", "eV", ev_)
        setfree("pu7", "eV", ev_)
        es_ = k.emit("scalar", lambda e: e.activation(R_v, R_v, AF.Ln),
                     waits=[("eV", ev_)], inc="eS")
        es_ = k.emit("scalar", lambda e: e.activation(R_v, R_v, AF.Exp, scale=-0.5),
                     waits=[("eS", es_)], inc="eS")
        ev_ = k.emit("vector", lambda e: e.tensor_mul(R_mr, R_mu, R_v),
                     waits=[("eS", es_)], inc="eV")
        k.emit("tensor", lambda e: e.matmul(pt(0), ONESR[:], R_v, start=True,
                                            stop=True),
               waits=[("eV", ev_)] + usew("pt01"))
        lastt = k.emit("tensor", lambda e: e.matmul(pt(1), ONESR[:], R_mr,
                                                    start=True, stop=True), inc="eT")
        setfree("stage", "eT", lastt)
        return ("eT", lastt)

    def phase_mix(c, tm_name, ready=None):
        ev_ = None
        for f in range(NF):
            par = f % 2
            k.emit("vector",
                   lambda e, f=f, par=par: e.tensor_tensor(
                       SQ(par)[:, 1:512], S1(f, 0, 511), S1(f, 1, 511), Alu.subtract),
                   waits=usew(f"sq{par}") + ([ready] if ready and f == 0 else []))
            k.emit("vector",
                   lambda e, f=f, par=par: e.tensor_tensor(
                       SQ(par)[:, 0:1], col("hl", f), S1(f, 0, 1), Alu.subtract))
            k.emit("vector",
                   lambda e, f=f, par=par, tm=tm_name: e.scalar_tensor_tensor(
                       SQ(par), SQ(par), col(tm, f), S1(f), Alu.mult, Alu.add))
            k.emit("vector", lambda e, f=f, par=par: e.tensor_copy(MhT := None or Mh(f), SQ(par)))
            ev_ = k.emit("vector",
                         lambda e, f=f, par=par: e.tensor_tensor(
                             Ml(f), SQ(par), Mh(f), Alu.subtract),
                         inc="eV")
            setfree(f"sq{par}", "eV", ev_)
        return ("eV", ev_)

    def phase_proj(c, wname, consumer, ready, moving_h, moving_l):
        for j in range(NF):
            par = j % 2
            sem = f"dA{par}"
            k.emit("sync",
                   lambda e, wn=wname, j=j, par=par: e.dma_start(
                       out=WhV(par)[:], in_=sw[wn][j][:, 0:2048]),
                   waits=usew(f"wb{par}") + [scratch_ready[(wname, j)]],
                   inc=sem, amt=16)
            nd = k.emit("sync",
                        lambda e, wn=wname, j=j, par=par: e.dma_start(
                            out=WlV(par)[:], in_=sw[wn][j][:, 2048:4096]),
                        inc=sem, amt=16)
            lastt = None
            nmm = 0
            for kc in range(NF):
                for wv, mv in ((WhV, moving_h), (WhV, moving_l), (WlV, moving_h)):
                    st_ = nmm == 0
                    sp_ = nmm == 3 * NF - 1
                    nmm += 1
                    lastt = k.emit("tensor",
                                   lambda e, j=j, kc=kc, par=par, wv=wv, mv=mv,
                                          st_=st_, sp_=sp_: e.matmul(
                                       pu(4 + par), wv(par)[:, kc * 128:(kc + 1) * 128],
                                       mv(kc), start=st_, stop=sp_),
                                   waits=([(sem, nd)] + usew(f"pu{4 + par}") +
                                          ([ready] if ready and j == 0 else []))
                                         if st_ else (),
                                   inc="eT" if sp_ else None)
            setfree(f"wb{par}", "eT", lastt)
            csem, cnt = consumer(j, pu(4 + par), lastt)
            setfree(f"pu{4 + par}", csem, cnt)

    def phase_scan(c, full, ek_done):
        ev_ = None
        for j in range(NF):
            wt = SQ(0)
            wt_ev = k.emit("vector",
                           lambda e, j=j, wt=wt: e.tensor_scalar(
                               wt[:], S4(0), ZEROC[:], col("w", j),
                               Alu.mult, Alu.add),
                           waits=usew("sq0") + ([ek_done] if j == 0 else []),
                           inc="eV")
            k.emit("vector", lambda e, j=j: e.tensor_copy(SCA[:, 0:1], col("cA", j)),
                   waits=[("eV", wt_ev)])
            k.emit("vector",
                   lambda e, j=j, wt=wt: e.tensor_tensor_scan(
                       SCA[:, 1:513], wt[:], S1(j), col("cA", j), Alu.mult, Alu.add),
                   inc="eV")
            k.emit("vector", lambda e, j=j: e.tensor_copy(SCB[:, 0:1], col("cB", j)))
            sc_ev = k.emit("vector",
                           lambda e, j=j, wt=wt: e.tensor_tensor_scan(
                               SCB[:, 1:513], wt[:], S4(j), col("cB", j),
                               Alu.mult, Alu.add),
                           inc="eV")
            k.emit("vector", lambda e, j=j: e.tensor_copy(col("cA", j),
                                                          SCA[:, 512:513]),
                   waits=[("eV", sc_ev)])
            ev_ = k.emit("vector",
                         lambda e, j=j: e.tensor_copy(col("cB", j), SCB[:, 512:513]),
                         inc="eV")
            setfree("sq0", "eV", ev_)
            if not full:
                continue
            num_ev = k.emit("vector",
                            lambda e, j=j: e.scalar_tensor_tensor(
                                SCA[:, 0:512], S1(j), col("eu", j), SCA[:, 0:512],
                                Alu.mult, Alu.add), inc="eV")
            ev_ = k.emit("vector",
                         lambda e, j=j: e.scalar_tensor_tensor(
                             SCB[:, 0:512], S4(j), col("eu", j), SCB[:, 0:512],
                             Alu.mult, Alu.add), inc="eV")
            es_ = k.emit("scalar", lambda e: e.activation(SCB[:, 0:512],
                                                          SCB[:, 0:512], AF.Ln),
                         waits=[("eV", ev_)], inc="eS")
            es_ = k.emit("scalar", lambda e: e.activation(SCB[:, 0:512],
                                                          SCB[:, 0:512], AF.Exp,
                                                          scale=-1.0),
                         waits=[("eS", es_)], inc="eS")
            k.emit("vector", lambda e: e.tensor_mul(SCA[:, 0:512], SCA[:, 0:512],
                                                    SCB[:, 0:512]),
                   waits=[("eS", es_), ("eV", num_ev)])
            k.emit("vector", lambda e, j=j: e.tensor_mul(SQ(1), SCA[:, 0:512], S3(j)))
            k.emit("vector", lambda e, j=j: e.tensor_copy(ATTh(j), SQ(1)))
            ev_ = k.emit("vector",
                         lambda e, j=j: e.tensor_tensor(ATTl(j), SQ(1), ATTh(j),
                                                        Alu.subtract),
                         inc="eV")
        if not full:
            k.emit("vector", lambda e: e.tensor_scalar_mul(COLS["cA"][:],
                                                           COLS["cA"][:], CFLAG[:]))
            ev_ = k.emit("vector", lambda e: e.tensor_scalar_mul(COLS["cB"][:],
                                                                 COLS["cB"][:],
                                                                 CFLAG[:]), inc="eV")
        return ("eV", ev_)

    def phase_store(c, x_ready):
        for tg in range(4):
            lastt = None
            for f in range(NF):
                dstp = PT[:, f * 128: f * 128 + 128]
                lastt = k.emit("tensor",
                               lambda e, f=f, tg=tg, d=dstp: e.transpose(
                                   d, X(f, tg * 128, 128), IDN[:]),
                               waits=(usew("pt01", "pt23") +
                                      ([x_ready] if tg == 0 else []))
                                     if f == 0 else (),
                               inc="eT" if f == NF - 1 else None)
            ev_ = k.emit("vector",
                         lambda e: e.tensor_copy(STG()[:, :], PT[:, 0:2048]),
                         waits=[("eT", lastt)] + usew("stage"), inc="eV")
            setfree("pt01", "eV", ev_)
            setfree("pt23", "eV", ev_)
            nd = k.emit("gpsimd",
                        lambda e, c=c, tg=tg: e.dma_start(
                            out=d_y[(c - 1) * TC + tg * 128:
                                    (c - 1) * TC + (tg + 1) * 128, :],
                            in_=STG()[:, :]),
                        waits=[("eV", ev_)], inc="dG", amt=16)
            setfree("stage", "dG", nd)

    def ln_apply(c, src_tile, bc_ready, emit_f):
        ev_ = None
        for f in range(NF):
            par = f % 2
            k.emit("vector",
                   lambda e, f=f, par=par: e.tensor_tensor(SQ(par), src_tile(f),
                                                           pt(0), Alu.mult),
                   waits=([bc_ready] if f == 0 else []) + usew(f"sq{par}"))
            k.emit("vector",
                   lambda e, f=f, par=par: e.tensor_tensor(SQ(par), SQ(par), pt(1),
                                                           Alu.subtract))
            ev_ = emit_f(f, par)
            setfree(f"sq{par}", "eV", ev_)
        setfree("pt01", "eV", ev_)
        return ("eV", ev_)

    scratch_ready = {}

    def prepass():
        groups = []
        for nm in ("wk", "wv", "wr", "wg", "wo"):
            for g in range(4):
                groups.append((nm, g, BF16))
        for g in range(16):
            groups.append(("w0", g, F16))
        for g in range(16):
            groups.append(("w1", g, F16))
        for gi, (nm, g, dt_) in enumerate(groups):
            par = gi % 2
            land = ar(OFF_S1 if par == 0 else OFF_S4, 8192)
            hi = ar(OFF_S2, 4096).bitcast(dt_)
            lo = ar(OFF_S3, 4096).bitcast(dt_)
            land3 = land.rearrange("p (j n) -> p j n", n=2048)
            nd = k.emit("sync",
                        lambda e, nm=nm, g=g, land=land, land3=land3: e.dma_start(
                            out=(land if nm == "w1" else land3),
                            in_=(wp[nm][g] if nm == "w1" else
                                 wp[nm].rearrange("j p n -> p j n")[:, g * 4:(g + 1) * 4, :])),
                        waits=usew(f"land{par}"), inc=f"dP{par}", amt=16)
            hs = k.emit("scalar",
                        lambda e, land=land, hi=hi: e.copy(hi[:], land),
                        waits=[(f"dP{par}", nd)] + usew("pphi"), inc="eS")
            ls = k.emit("vector",
                        lambda e, land=land, hi=hi, lo=lo: e.tensor_tensor(
                            lo[:], land, hi[:], Alu.subtract),
                        waits=[("eS", hs)] + usew("pplo"), inc="eV")
            setfree(f"land{par}", "eV", ls)
            if nm == "w1":
                dsth = sw["w1"][g].rearrange("q p n -> p q n")[:, :, 0:2048]
                dstl = sw["w1"][g].rearrange("q p n -> p q n")[:, :, 2048:4096]
            else:
                dsth = sw[nm].rearrange("j p n -> p j n")[:, g * 4:(g + 1) * 4, 0:2048]
                dstl = sw[nm].rearrange("j p n -> p j n")[:, g * 4:(g + 1) * 4, 2048:4096]
            hi3 = hi[:].rearrange("p (j n) -> p j n", n=2048)
            lo3 = lo[:].rearrange("p (j n) -> p j n", n=2048)
            k.emit("gpsimd",
                   lambda e, hi3=hi3, dsth=dsth: e.dma_start(out=dsth, in_=hi3),
                   waits=[("eV", ls)], inc="dG", amt=16)
            ndo = k.emit("gpsimd",
                         lambda e, lo3=lo3, dstl=dstl: e.dma_start(out=dstl, in_=lo3),
                         inc="dG", amt=16)
            setfree("pphi", "dG", ndo)
            setfree("pplo", "dG", ndo)
            if nm == "w1":
                for qt in range(4):
                    scratch_ready[("w1", g, qt)] = ("dG", ndo)
            else:
                for jj in range(g * 4, g * 4 + 4):
                    scratch_ready[(nm, jj)] = ("dG", ndo)

    def dbg_dump(off):
        nd = k.emit("gpsimd",
               lambda e: e.dma_start(out=d_dbg[:, :], in_=AR[:, off:off + 8192]),
               waits=[("eV", k.now("eV")), ("eS", k.now("eS")),
                      ("eT", k.now("eT"))], inc="dG", amt=16)
        for eng_ in ("vector", "scalar", "tensor", "sync"):
            k.emit(eng_, lambda e: None, waits=[("dG", nd)], noinc=True)

    def dbg_dump_cols(names):
        for i, nm in enumerate(names):
            k.emit("gpsimd",
                   lambda e, i=i, nm=nm: e.dma_start(
                       out=d_dbg[:, i * 16:(i + 1) * 16], in_=COLS[nm][:]),
                   waits=[("eV", k.now("eV")), ("eS", k.now("eS"))],
                   inc="dG", amt=16)

    if DBG == ("cols", -1):
        dbg_dump_cols(("sa", "ca", "gax", "sb", "cb", "gbx", "w", "eu", "sil",
                       "ca1", "cb1", "gag", "gab"))

    prepass()
    pp_done = ("dG", k.now("dG"))
    k.emit("vector", lambda e: None, waits=[pp_done], noinc=True)
    k.emit("scalar", lambda e: None, waits=[pp_done], noinc=True)

    ek_done = [None]
    for c in range(NCHUNK):
        full = c > 0
        x_loaded = phase_load_x(c)

        bc = phase_ln(c, lambda f: X(f),
                      ready=[x_loaded] + ([("eV", setup_ev)] if c == 0 else []))
        if DBG == ("x", c):
            dbg_dump(OFF_X)
        h_ev = ln_apply(c, lambda f: X(f), bc,
                        lambda f, par: k.emit("vector",
                                              lambda e, f=f, par=par: e.tensor_scalar(
                                                  S1(f), SQ(par), col("ca1", f),
                                                  col("sa", f), Alu.mult, Alu.add),
                                              inc="eV"))

        def cons_r(j, bank, tcnt):
            cnt = k.emit("scalar",
                         lambda e, j=j, bank=bank: e.activation(S3(j), bank,
                                                                AF.Sigmoid),
                         waits=[("eT", tcnt)], inc="eS")
            return ("eS", cnt)

        def cons_g(j, bank, tcnt):
            cs = k.emit("scalar",
                        lambda e, bank=bank: e.activation(GT[:], bank, AF.Sigmoid),
                        waits=[("eT", tcnt)] + usew("gt"), inc="eS")
            cv = k.emit("vector", lambda e, j=j: e.tensor_mul(S3(j), S3(j), GT[:]),
                        waits=[("eS", cs)], inc="eV")
            setfree("gt", "eV", cv)
            return ("eS", cs)

        def cons_k(j, bank, tcnt):
            cnt = k.emit("scalar",
                         lambda e, j=j, bank=bank: e.activation(S4(j), bank, AF.Exp),
                         waits=[("eT", tcnt)], inc="eS")
            return ("eS", cnt)

        def cons_v(j, bank, tcnt):
            cnt = k.emit("vector",
                         lambda e, j=j, bank=bank: e.tensor_tensor(S1(j), S4(j),
                                                                   bank, Alu.mult),
                         waits=[("eT", tcnt)] + ([ek_done[0]] if j == 0 else []),
                         inc="eV")
            return ("eV", cnt)

        if DBG == ("h", c):
            dbg_dump(OFF_S1)
        if full:
            mev = phase_mix(c, "tmr", ready=h_ev)
            phase_proj(c, "wr", cons_r, mev, Mh, Ml)
            mev = phase_mix(c, "tmg", ready=("eT", k.now("eT")))
            phase_proj(c, "wg", cons_g, mev, Mh, Ml)
        mev = phase_mix(c, "tmk",
                        ready=("eT", k.now("eT")) if full else h_ev)
        if DBG == ("mixk", c):
            dbg_dump(OFF_S2)
        phase_proj(c, "wk", cons_k, mev, Mh, Ml)
        ek_done[0] = ("eS", k.now("eS"))
        if DBG == ("ek", c):
            dbg_dump(OFF_S4)
        mev = phase_mix(c, "tmv", ready=("eT", k.now("eT")))
        k.emit("vector", lambda e: e.tensor_copy(
            COLS["hl"][:], AR[:, OFF_S1 + 511: OFF_S1 + 8192: 512]))
        phase_proj(c, "wv", cons_v, mev, Mh, Ml)

        if DBG == ("ekv", c):
            dbg_dump(OFF_S1)
        if DBG == ("sg", c):
            dbg_dump(OFF_S3)
        scan_ev = phase_scan(c, full, ek_done[0])
        if DBG == ("carry", c):
            for i_, nm_ in ((0, "cA"), (1, "cB")):
                k.emit("gpsimd",
                       lambda e, i_=i_, nm_=nm_: e.dma_start(
                           out=d_dbg[:, i_ * 16:(i_ + 1) * 16], in_=COLS[nm_][:]),
                       waits=[("eV", k.now("eV"))], inc="dG", amt=16)
            k.emit("vector", lambda e: None, waits=[("dG", k.now("dG"))])
        if not full:
            continue

        if DBG == ("att", c):
            dbg_dump(OFF_S1)
        o_cnt = [0]

        def cons_o(j, bank, tcnt):
            cnt = k.emit("scalar", lambda e, j=j, bank=bank: e.copy(S2(j), bank),
                         waits=[("eT", tcnt)], inc="eS")
            o_cnt[0] = cnt
            return ("eS", cnt)

        phase_proj(c, "wo", cons_o, scan_ev, ATTh, ATTl)

        if DBG == ("obuf", c):
            dbg_dump(OFF_S2)
        bc = phase_ln(c, lambda f: S2(f), ready=[("eS", o_cnt[0])])
        if DBG == ("rowso", c):
            nd_ = k.emit("gpsimd",
                         lambda e: e.dma_start(out=d_dbg[:, 0:2048],
                                               in_=AR[:, OFF_STAGE:OFF_STAGE + 2048]),
                         waits=[("eV", k.now("eV")), ("eS", k.now("eS")),
                                ("eT", k.now("eT"))], inc="dG", amt=16)
            for eng_ in ("vector", "scalar", "tensor", "sync"):
                k.emit(eng_, lambda e: None, waits=[("dG", nd_)], noinc=True)
        if DBG == ("xpre", c):
            dbg_dump(OFF_X)

        def fin_o(f, par):
            n_ = k.emit("vector",
                        lambda e, f=f, par=par: e.tensor_scalar(
                            SQ(par), SQ(par), col("gag", f), col("gab", f),
                            Alu.mult, Alu.add), inc="eV")
            return k.emit("vector",
                          lambda e, f=f, par=par: e.tensor_tensor(X(f), X(f),
                                                                  SQ(par), Alu.add),
                          waits=[("eV", n_)], inc="eV")

        x_attn = ln_apply(c, lambda f: S2(f), bc, fin_o)
        if DBG == ("bco", c):
            nv_ = k.emit("vector", lambda e: e.tensor_copy(
                AR[:, OFF_STAGE:OFF_STAGE + 512], pt(0)), inc="eV")
            nv_ = k.emit("vector", lambda e: e.tensor_copy(
                AR[:, OFF_STAGE + 512:OFF_STAGE + 1024], pt(1)), inc="eV")
            nd_ = k.emit("gpsimd",
                         lambda e: e.dma_start(out=d_dbg[:, 0:2048],
                                               in_=AR[:, OFF_STAGE:OFF_STAGE + 2048]),
                         waits=[("eV", nv_)], inc="dG", amt=16)
            for eng_ in ("vector", "scalar", "tensor", "sync"):
                k.emit(eng_, lambda e: None, waits=[("dG", nd_)], noinc=True)
        if DBG == ("obuf2", c):
            dbg_dump(OFF_S2)
        if DBG == ("rowso2", c):
            nd_ = k.emit("gpsimd",
                         lambda e: e.dma_start(out=d_dbg[:, 0:2048],
                                               in_=AR[:, OFF_STAGE:OFF_STAGE + 2048]),
                         waits=[("eV", k.now("eV")), ("eS", k.now("eS")),
                                ("eT", k.now("eT"))], inc="dG", amt=16)
            for eng_ in ("vector", "scalar", "tensor", "sync"):
                k.emit(eng_, lambda e: None, waits=[("dG", nd_)], noinc=True)

        if DBG == ("xattn", c):
            dbg_dump(OFF_X)
        bc = phase_ln(c, lambda f: X(f), ready=[x_attn])

        def fin_h2(f, par):
            return k.emit("vector",
                          lambda e, f=f, par=par: e.tensor_scalar(
                              H2F(f), SQ(par), col("cb1", f), col("sb", f),
                              Alu.mult, Alu.add),
                          inc="eV")

        h2_ev = ln_apply(c, lambda f: X(f), bc, fin_h2)

        if DBG == ("h2", c):
            dbg_dump(OFF_S4)
        for nm in ("wf0", "wf1"):
            freed[nm] = x_attn
        last_cs = None
        for m in range(NH):
            par = m % 2
            sem = f"dF{par}"
            k.emit("sync",
                   lambda e, m=m, par=par: e.dma_start(
                       out=FWH(par)[:], in_=sw["w0"][m][:, 0:2048]),
                   waits=usew(f"wf{par}") + [scratch_ready[("w0", m // 4 * 4)]],
                   inc=sem, amt=16)
            nd = k.emit("sync",
                        lambda e, m=m, par=par: e.dma_start(
                            out=FWL(par)[:], in_=sw["w0"][m][:, 2048:4096]),
                        inc=sem, amt=16)
            lastt = None
            for p_, wreg in ((0, FWH), (1, FWL)):
                for kc in range(NF):
                    st_ = (p_ == 0 and kc == 0)
                    sp_ = (p_ == 1 and kc == NF - 1)
                    lastt = k.emit("tensor",
                                   lambda e, kc=kc, par=par, wreg=wreg, st_=st_,
                                          sp_=sp_: e.matmul(
                                       pu(4 + par), wreg(par)[:, kc * 128:(kc + 1) * 128],
                                       H2F(kc), start=st_, stop=sp_),
                                   waits=([(sem, nd)] + usew(f"pu{4 + par}") +
                                          ([h2_ev] if m == 0 else []))
                                         if st_ else (),
                                   inc="eT" if sp_ else None)
            setfree(f"wf{par}", "eT", lastt)
            last_cs = k.emit("scalar",
                             lambda e, m=m, par=par: e.activation(
                                 DF(m), pu(4 + par), AF.Gelu, bias=B0C[:, m:m + 1]),
                             waits=[("eT", lastt)], inc="eS")
            setfree(f"pu{4 + par}", "eS", last_cs)
        d_ready = ("eS", last_cs)

        if DBG == ("dff", c):
            dbg_dump(OFF_S1)
        if DBG == ("dff2", c):
            dbg_dump(OFF_S3)
        for j in range(NF):
            parj = j % 2
            lastt = None
            for qt in range(4):
                qpar = qt % 2
                k.emit("sync",
                       lambda e, j=j, qt=qt, qpar=qpar: e.dma_start(
                           out=FWH(qpar)[:], in_=sw["w1"][j, qt][:, 0:2048]),
                       waits=usew(f"wf{qpar}") + [scratch_ready[("w1", j, qt)]],
                       inc=f"dQ{qpar}", amt=16)
                nd = k.emit("sync",
                            lambda e, j=j, qt=qt, qpar=qpar: e.dma_start(
                                out=FWL(qpar)[:], in_=sw["w1"][j, qt][:, 2048:4096]),
                            inc=f"dQ{qpar}", amt=16)
                for p_, wreg in ((0, FWH), (1, FWL)):
                    for kc in range(NF):
                        m = qt * NF + kc
                        st_ = (qt == 0 and p_ == 0 and kc == 0)
                        sp_ = (qt == 3 and p_ == 1 and kc == NF - 1)
                        qlast = (p_ == 1 and kc == NF - 1)
                        lastt = k.emit("tensor",
                                       lambda e, m=m, kc=kc, qpar=qpar,
                                              wreg=wreg, parj=parj, st_=st_,
                                              sp_=sp_: e.matmul(
                                           pu(6 + parj),
                                           wreg(qpar)[:, kc * 128:(kc + 1) * 128],
                                           DF(m), start=st_, stop=sp_),
                                       waits=([(f"dQ{qpar}", nd)] +
                                              (usew(f"pu{6 + parj}") +
                                               ([d_ready] if j == 0 else [])
                                               if qt == 0 else []))
                                             if p_ == 0 and kc == 0 else (),
                                       inc="eT" if qlast else None)
                setfree(f"wf{qpar}", "eT", lastt)
            cs = k.emit("vector",
                        lambda e, j=j, parj=parj: e.tensor_single_scalar(
                            E16[:], pu(6 + parj), col("b1", j), Alu.add),
                        waits=[("eT", lastt)] + usew("e16"), inc="eV")
            setfree(f"pu{6 + parj}", "eV", cs)
            if DBG == ("e16", c) and j == 0:
                nde_ = k.emit("gpsimd",
                              lambda e: e.dma_start(out=d_dbg[:, 0:256],
                                                    in_=ar(0, 0) if False else E16[:].bitcast(F32)),
                              waits=[("eV", cs)], inc="dG", amt=16)
                for eng_ in ("vector", "scalar", "tensor", "sync"):
                    k.emit(eng_, lambda e: None, waits=[("dG", nde_)], noinc=True)
            cv = k.emit("vector",
                        lambda e, j=j: e.scalar_tensor_tensor(
                            X(j), E16[:], col("gbx", j), X(j), Alu.mult, Alu.add),
                        waits=[("eV", cs)], inc="eV")
            setfree("e16", "eV", cv)
        x_final = ("eV", cv)

        if DBG == ("xfin", c):
            dbg_dump(OFF_X)
        phase_store(c, x_final)

    k.emit("gpsimd", lambda e: None, waits=[("dG", k.now("dG"))], noinc=True)
    k.emit("sync", lambda e: None, waits=[("dG", k.now("dG"))], noinc=True)

    with nc.Block() as block:
        @block.sync
        def _(eng):
            k.replay("sync", eng, sems)

        @block.gpsimd
        def _(eng):
            k.replay("gpsimd", eng, sems)

        @block.tensor
        def _(eng):
            k.replay("tensor", eng, sems)

        @block.vector
        def _(eng):
            k.replay("vector", eng, sems)

        @block.scalar
        def _(eng):
            k.replay("scalar", eng, sems)

    st.close()
    return nc


def _pack_w(wt):
    K, J = wt.shape
    return np.ascontiguousarray(
        wt.reshape(K // 128, 128, J // 128, 128).transpose(2, 1, 0, 3)
        .reshape(J // 128, 128, K))


def _cols(v):
    return np.ascontiguousarray(np.asarray(v, dtype=np.float32).reshape(-1, 128).T)


def kernel(L0_attn_x, L0_attn_time_decay, L0_attn_time_first, L0_attn_tm_k,
           L0_attn_tm_v, L0_attn_tm_r, L0_attn_tm_g, L0_attn_Wr, L0_attn_Wk,
           L0_attn_Wv, L0_attn_Wg, L0_attn_Wo, L0_attn_ln_g, L0_attn_ln_b,
           L0_ffn_W0, L0_ffn_bias0, L0_ffn_W1, L0_ffn_bias1, L0_c, L0_wmod):
    x = np.asarray(L0_attn_x, dtype=np.float32)
    shared = {
        "wr": _pack_w(np.ascontiguousarray(np.asarray(L0_attn_Wr, dtype=np.float32).T)),
        "wk": _pack_w(np.ascontiguousarray(np.asarray(L0_attn_Wk, dtype=np.float32).T)),
        "wv": _pack_w(np.ascontiguousarray(np.asarray(L0_attn_Wv, dtype=np.float32).T)),
        "wg": _pack_w(np.ascontiguousarray(np.asarray(L0_attn_Wg, dtype=np.float32).T)),
        "wo": _pack_w(np.ascontiguousarray(np.asarray(L0_attn_Wo, dtype=np.float32).T)),
        "w0": _pack_w(np.ascontiguousarray(np.asarray(L0_ffn_W0, dtype=np.float32).T)),
        "w1": _pack_w(np.ascontiguousarray(np.asarray(L0_ffn_W1, dtype=np.float32).T)),
        "wmod": np.ascontiguousarray(np.asarray(L0_wmod, dtype=np.float32)),
        "td": _cols(L0_attn_time_decay),
        "tf": _cols(L0_attn_time_first),
        "tmk": _cols(L0_attn_tm_k),
        "tmv": _cols(L0_attn_tm_v),
        "tmr": _cols(L0_attn_tm_r),
        "tmg": _cols(L0_attn_tm_g),
        "lng": _cols(L0_attn_ln_g),
        "lnb": _cols(L0_attn_ln_b),
        "b1": _cols(L0_ffn_bias1),
        "b0": _cols(L0_ffn_bias0),
        "idn": np.eye(128, dtype=np.float32),
    }
    in_maps = []
    for core in range(NCORES):
        b, q = divmod(core, 4)
        xs = np.zeros((ROWS, C), dtype=np.float32)
        if q == 0:
            xs[511] = x[b, 1]
            xs[512:] = x[b, 0:2048]
            cflag = np.zeros((128, 1), dtype=np.float32)
        else:
            xs[:] = x[b, q * 2048 - 512: q * 2048 + 2048]
            cflag = np.ones((128, 1), dtype=np.float32)
        im = dict(shared)
        im["xs"] = xs
        im["cflag"] = cflag
        im["crow"] = np.ascontiguousarray(
            np.asarray(L0_c, dtype=np.float32)[b:b + 1])
        in_maps.append(im)

    nc = build_program()
    res = run_bass_kernel_spmd(nc, in_maps, core_ids=list(range(NCORES)))
    out = np.empty((BATCH, T, C), dtype=np.float32)
    for core in range(NCORES):
        b, q = divmod(core, 4)
        out[b, q * 2048:(q + 1) * 2048] = res.results[core]["y"]
    return out
